# version 05
# baseline (speedup 1.0000x reference)
import numpy as np
import contextlib
import concourse.bass as bass
import concourse.mybir as mybir
from concourse.bass_utils import run_bass_kernel_spmd

F32 = mybir.dt.float32
BF16 = mybir.dt.bfloat16
I32 = mybir.dt.int32
A = mybir.AluOpType
AF = mybir.ActivationFunctionType
AP = bass.AP

B, T, E, H, D = 4, 4096, 1024, 16, 64
C = 128
NCH = 32
NOUR = 16
NCORES = 8

_TABLES = {}
_NC_CACHE = {}


def _host_tables():
    if _TABLES:
        return _TABLES
    import jax
    with jax.default_device(jax.devices('cpu')[0]):
        import jax.numpy as jnp
        inv_freq = 1.0 / 10000.0 ** (jnp.arange(0, D, 2, dtype=jnp.float32) / D)
        t = jnp.arange(T, dtype=jnp.float32)
        freqs = t[:, None] * inv_freq[None, :]
        emb = jnp.concatenate([freqs, freqs], axis=-1)
        _TABLES['cos'] = np.asarray(jnp.cos(emb))
        _TABLES['sin'] = np.asarray(jnp.sin(emb))
    s2 = _TABLES['sin'].copy()
    s2[:, 0::2] *= -1.0
    _TABLES['s2'] = s2
    return _TABLES


def _core_inputs(core, x_full, W, bvec, gamma, Wf, bf_):
    tb = _host_tables()
    b = core // 2
    half = core % 2
    perm = np.concatenate([np.arange((1 - half) * 2048, (2 - half) * 2048),
                           np.arange(half * 2048, (half + 1) * 2048)])
    xl = np.ascontiguousarray(x_full[b][perm].astype(np.float32))
    cos_l = tb['cos'][perm]
    s2_l = tb['s2'][perm]

    def tm_table(tab):
        r = tab.reshape(NCH, C, D).transpose(1, 0, 2).reshape(C, NCH * D)
        return np.ascontiguousarray(r.astype(np.float32))

    def fm_table(tab, scale):
        ours = tab[NOUR * C:]
        oc = ours.reshape(NOUR, C, D)
        r = np.empty((128, NOUR * C), np.float32)
        for p in range(128):
            r[p] = (oc[:, :, p % 64] * scale).reshape(-1)
        return r

    d = {
        'x': xl,
        'xT': np.ascontiguousarray(xl.T),
        'cos_tm': tm_table(cos_l), 's2_tm': tm_table(s2_l),
        'cos_fm': fm_table(cos_l, 0.125), 's2_fm': fm_table(s2_l, 0.125),
        'gam': np.asarray(gamma).reshape(1, H).astype(np.float32),
    }
    for nm, w in [('wq', W[0]), ('wk', W[1]), ('wv', W[2]), ('wg', W[3]), ('wf', Wf)]:
        d[nm + '_t'] = np.ascontiguousarray(np.asarray(w).T.astype(np.float32))
    for nm, v in [('bq', bvec[0]), ('bk', bvec[1]), ('bv', bvec[2]),
                  ('bg', bvec[3]), ('bf', bf_)]:
        d[nm + '_r'] = np.asarray(v).reshape(1, E).astype(np.float32)
    j = np.arange(C, dtype=np.float32)
    dk = np.empty((C, 3), np.float32)
    for v in range(2):
        dk[:, v] = (256.0 - v * 128.0 - j) if half == 1 else 1e30
    dk[:, 2] = 128.0 - j
    d['dk_arg'] = dk
    d['fmask'] = np.full((C, 1), 1.0 if half == 0 else 0.0, np.float32)
    d['iota_t'] = np.tile(j.reshape(1, C), (C, 1)).astype(np.float32)
    d['dmatT'] = np.maximum(j[None, :] - j[:, None], 0.0).astype(np.float32)
    d['ident'] = np.eye(C, dtype=np.float32)
    ps = np.zeros((C, C), np.float32)
    for a_ in range(C):
        ps[a_, a_ ^ 1] = 1.0
    d['pswap'] = ps
    M1 = np.zeros((H, C), np.float32)
    for k in range(H):
        M1[k, (k % 2) * 64:(k % 2) * 64 + 64] = 1.0
    d['m1'] = M1
    Hm = np.zeros((H, 8), np.float32)
    for k in range(H):
        Hm[k, k // 2] = 1.0
    d['hm'] = Hm
    return d


def build_nc(s1=NCH, s2=True, s2n=NOUR, s2ops=None, dbg_x1=False):
    nc = bass.Bass("TRN2", target_bir_lowering=False)
    h = {}

    def di(name, shape, dt=F32):
        h[name] = nc.dram_tensor(name, list(shape), dt, kind="ExternalInput").ap()
    di('x', (T, E)); di('xT', (E, T))
    di('cos_tm', (C, NCH * D)); di('s2_tm', (C, NCH * D))
    di('cos_fm', (C, NOUR * C)); di('s2_fm', (C, NOUR * C))
    di('gam', (1, H))
    for nm in ['wq', 'wk', 'wv', 'wg', 'wf']:
        di(nm + '_t', (E, E))
    for nm in ['bq', 'bk', 'bv', 'bg', 'bf']:
        di(nm + '_r', (1, E))
    di('dk_arg', (C, 3)); di('fmask', (C, 1)); di('iota_t', (C, C))
    di('dmatT', (C, C)); di('ident', (C, C)); di('pswap', (C, C))
    di('m1', (H, C)); di('hm', (H, 8))
    outh = nc.dram_tensor("out", [NOUR * C, E], F32, kind="ExternalOutput").ap()

    es = contextlib.ExitStack()
    with es:
        def sb(name, shape, dt=BF16):
            return es.enter_context(nc.sbuf_tensor(name, list(shape), dt))

        def psum(name, shape):
            return es.enter_context(nc.psum_tensor(name, list(shape), F32))

        wA = sb('wA', (C, 8 * E))
        wB = sb('wB', (C, 8 * E))
        wC = sb('wC', (C, 8 * E))
        krot_st = sb('krot_st', (C, NOUR * E))
        v_st = sb('v_st', (C, NOUR * E))
        ssnap = sb('ssnap', (C, NOUR * 512))
        srun = sb('srun', (C, 512), F32)
        fsb = sb('fsb', (C, 512), F32)
        costab = sb('costab', (C, 1024), F32)
        cfm32 = sb('cfm32', (C, 1024), F32)
        ovl1 = sb('ovl1', (C, 3 * E))
        a128 = sb('a128', (C, 512)); a256 = sb('a256', (C, 512))
        wintraT = sb('wintraT', (C, H * C))
        ident = sb('ident_s', (C, C)); pswap = sb('pswap_s', (C, C))
        iota_t = sb('iota_s', (C, C)); dmatT = sb('dmatT_s', (C, C))
        dkcol = sb('dkcol', (C, 3), F32); fmcol = sb('fmcol', (C, 1), F32)
        ones1 = sb('ones1', (1, C)); onec = sb('onec', (C, 1))
        browk = sb('browk', (1, E)); browv = sb('browv', (1, E))
        csumk = sb('csumk', (1, E)); csumv = sb('csumv', (1, E))
        browq = browk; browg = browv; csumq = csumk; csumg = csumv
        stats = sb('stats', (C, NCH * 3), F32)
        statbf = sb('statbf', (C, NCH * 3))
        gamr = sb('gamr', (1, H), F32)
        lam16 = sb('lam16', (1, H), F32)
        lamr_bf = csumv[0:1, :]
        lam16bf = sb('lam16bf', (1, H))
        lamfmc = sb('lamfmc', (C, 8), F32)
        m1s = sb('m1s', (H, C)); hms = sb('hms', (H, 8), F32)
        rhs16 = sb('rhs16', (H, 8))
        lcol = sb('lcol', (H, 1), F32)
        epsb = sb('epsb', (C, 1), F32)
        statT = sb('statT', (1, 3 * C))
        rbcfm = sb('rbcfm', (C, C), F32)
        accv = sb('accv', (C, 1), F32); accv2 = sb('accv2', (C, 1), F32)
        tiny = sb('tinyv', (C, 8), F32)

        pbf = sb('pbf', (C, 12288))
        pi = sb('pi', (C, 2048), I32)
        pf = sb('pf', (C, 3072), F32)

        xfm = [pbf[:, 0:1024], pbf[:, 0:1024]]
        tc1 = pbf[:, 1024:2048]
        tc2 = pbf[:, 2048:3072]
        xtmb = pbf[:, 3072:4096]
        ktm = pbf[:, 5120:6144]
        sq_dump = ktm
        kshuf = pbf[:, 6144:7168]
        kdec = pbf[:, 7168:8192]
        kmsk = pbf[:, 8192:9216]
        qfm_raw = pbf[:, 3072:4096]
        qrot = pbf[:, 4096:5120]
        qdfm = pbf[:, 5120:6144]
        kfm = pbf[:, 6144:7168]
        gate = pbf[:, 7168:8192]
        ptw = pbf[:, 8192:9216]
        fev = pbf[:, 9216:9728]
        qxb = pbf[:, 9728:10752]
        ffnq = pbf[:, 10752:11776]
        bq_t = pbf[:, 11776:12288]
        fev_lo = bq_t[:, 0:512]
        qlo = pbf[:, 1024:2048]
        graw = pbf[:, 7168:8192]
        thg = tc2

        x1 = pf[:, 0:1024]
        uc = pf[:, 1024:1536]
        u2 = pf[:, 1536:2048]
        xtmf = pf[:, 2048:3072]
        vq = xtmf
        lamtm = pf[:, 2048:3072]
        setup_arg = pf[:, 0:2048]
        wsc_f = pf[:, 0:1024]
        fold_t = pf[:, 1024:1536]

        ei_i = pi[:, 0:512]; e2_i = pi[:, 512:1024]; t3_i = pi[:, 1024:1536]
        t4_i = pi[:, 1536:2048]
        qo_i = ei_i

        p_proj = psum('p_proj', (C, 1024))
        p_sh = psum('p_sh', (C, 1024))
        p_out = psum('p_out', (C, 1024))
        p_F = psum('p_F', (C, 1024))

        sems = {}
        for s in ['t', 'v', 's', 'g', 'o', 'const', 'xf', 'xtb', 'xtf',
                  'wA', 'wB', 'wC', 'wf', 'rowq', 'rowg', 'bfrow', 'cfm']:
            sems[s] = es.enter_context(nc.semaphore('sem_' + s))

        plan = {k: [] for k in ['sy', 't', 'v', 's', 'g']}
        cnt = {k: 0 for k in ['t', 'v', 's', 'g', 'o']}
        marks = {}

        def op(eng, fn, waits=(), key=None, dma=None, osem=False):
            lst = plan['g'] if eng in ('g', 'gd') else plan[eng]
            if osem:
                semk = 'o'
            elif dma is not None:
                semk = dma
            else:
                semk = 'g' if eng == 'gd' else eng
            inc = 16 if (dma or osem) else 1
            item = {'fn': fn, 'waits': [marks[w] for w in waits], 'inc': None}
            if key or dma or osem:
                cnt.setdefault(semk, 0)
                cnt[semk] += inc
                item['inc'] = (semk, inc)
                if key:
                    marks[key] = (semk, cnt[semk])
            lst.append(item)

        def zs_free(base_ap, pre, reps, inner):
            return AP(tensor=base_ap.tensor, offset=base_ap.offset + pre,
                      ap=[list(base_ap.ap[0]), [0, reps], [1, inner]])

        def fn(g):
            return g.dma_start(out=costab[:, 0:512], in_=h['cos_tm'][:, 0:512])
        op('gd', fn, key='d_cosq0a', dma='const')
        def fn(g):
            return g.dma_start(out=costab[:, 512:1024], in_=h['s2_tm'][:, 0:512])
        op('gd', fn, key='d_cosq0', dma='const')
        for nm, dst in [('ident', ident),
                        ('pswap', pswap), ('m1', m1s), ('iota_t', iota_t),
                        ('dmatT', dmatT)]:
            def fn(g, nm=nm, dst=dst):
                return g.dma_start(out=dst[:], in_=h[nm])
            op('gd', fn, key='d_' + nm, dma='const')
        for nm, dst in [('dk_arg', dkcol), ('fmask', fmcol), ('gam', gamr),
                        ('hm', hms)]:
            def fn(g, nm=nm, dst=dst):
                return g.dma_start(out=dst[:], in_=h[nm])
            op('gd', fn, key='d_' + nm, dma='const')
        for nm, dst in [('bk', browk), ('bv', browv)]:
            def fn(g, nm=nm, dst=dst):
                return g.dma_start(out=dst[:], in_=h[nm + '_r'])
            op('gd', fn, key='d_' + nm, dma='const')

        def wload(g, slot, src):
            src_ap = AP(tensor=src.tensor, offset=src.offset,
                        ap=[[E, C], [C * E, 8], [1, E]])
            return g.dma_start(out=slot[:].rearrange("p (a b) -> p a b", a=8),
                               in_=src_ap)
        _ctot = cnt.get('const', 0)
        for _mk in list(marks):
            if marks[_mk][0] == 'const':
                marks[_mk] = ('const', _ctot)
        op('gd', lambda g: wload(g, wA, h['wk_t']), key='d_wk', dma='wA')
        op('gd', lambda g: wload(g, wB, h['wv_t']), key='d_wv', dma='wB')

        def fn(v):
            v.memset(ones1[:], 1.0)
            v.memset(onec[:], 1.0)
            v.memset(epsb[:], 1e-5)
            v.memset(fsb[:], 0.0)
            return v.memset(srun[:], 0.0)
        op('v', fn, key='v_init')

        def fn(s):
            return s.activation(lam16[:], gamr[:], AF.Tanh, scale=0.5)
        op('s', fn, waits=['d_gam'], key='s_lam0')

        def fn(v):
            v.tensor_scalar(lam16[:], lam16[:], 0.5, 0.5, A.mult, A.add)
            l16 = lam16[:]
            v.tensor_copy(lamr_bf.rearrange("p (a b) -> p a b", a=H),
                          AP(tensor=l16.tensor, offset=l16.offset,
                             ap=[list(l16.ap[0]), [1, H], [0, D]]))
            return v.tensor_copy(lam16bf[:], lam16[:])
        op('v', fn, waits=['s_lam0', 'v_init'], key='v_lam')

        def fn(t):
            return t.matmul(p_sh[0:H, 0:1], lam16bf[:], ones1[:, 0:1],
                            start=True, stop=True)
        op('t', fn, waits=['v_lam'], key='t_lcol')

        def fn(s):
            return s.copy(lcol[:], p_sh[0:H, 0:1])
        op('s', fn, waits=['t_lcol'], key='s_lcol')

        def fn(t):
            t.matmul(p_sh[:, 0:512], ones1[:], lamr_bf[0:1, 0:512], start=True, stop=True)
            return t.matmul(p_sh[:, 512:1024], ones1[:], lamr_bf[0:1, 512:1024],
                            start=True, stop=True)
        op('t', fn, waits=['s_lcol'], key='t_lamtm')

        def fn(s):
            return s.copy(lamtm, p_sh[:])
        op('s', fn, waits=['t_lamtm'], key='s_lamtm')

        def fn(v):
            return v.tensor_scalar(rhs16[:], hms[:], lcol[:], None, A.mult)
        op('v', fn, waits=['s_lcol', 'd_hm'], key='v_rhs16')

        def fn(t):
            return t.matmul(p_sh[:, 0:8], m1s[:], rhs16[:], start=True, stop=True)
        op('t', fn, waits=['v_rhs16', 'd_m1', 's_lamtm'], key='t_lamfm')

        def fn(s):
            return s.copy(lamfmc[:], p_sh[:, 0:8])
        op('s', fn, waits=['t_lamfm'], key='s_lamfm')

        def fn(t):
            r16 = rhs16[:]
            rhs = AP(tensor=r16.tensor, offset=r16.offset,
                     ap=[[8, H], [1, 8], [0, D]])
            return t.matmul(p_sh[:, 0:512], m1s[:], rhs, start=True, stop=True)
        op('t', fn, waits=['s_lamfm'], key='t_lamc')

        def fn(s):
            s.activation(a128[:], p_sh[:, 0:512], AF.Exp, scale=-128.0)
            return s.activation(a256[:], p_sh[:, 0:512], AF.Exp, scale=-256.0)
        op('s', fn, waits=['t_lamc'], key='s_atiles')

        for vnum in range(3):
            def fn(v, vnum=vnum):
                return v.tensor_scalar(setup_arg[:, 0:1024], lamtm,
                                       dkcol[:, vnum:vnum + 1], None, A.mult)
            op('v', fn, waits=['s_lamtm', 'd_dk_arg'] +
               ([f's_dk{vnum-1}'] if vnum else []), key=f'v_dk{vnum}')

            def fn(s, vnum=vnum):
                return s.activation(ovl1[:, vnum * E:(vnum + 1) * E],
                                    setup_arg[:, 0:1024], AF.Exp, scale=-1.0)
            op('s', fn, waits=[f'v_dk{vnum}'], key=f's_dk{vnum}')

        def fn(v):
            dmb = dmatT[:]
            pfb = pf[:]
            r = None
            for b2 in range(2):
                outap = AP(tensor=setup_arg.tensor,
                           offset=setup_arg.offset + b2 * 512,
                           ap=[[3072, C], [1024, 2], [C, 4], [1, C]])
                dm = AP(tensor=dmb.tensor, offset=dmb.offset,
                        ap=[[C, C], [0, 2], [0, 4], [1, C]])
                lm = AP(tensor=pfb.tensor,
                        offset=pfb.offset + 2048 + b2 * D,
                        ap=[[3072, C], [8 * D, 2], [2 * D, 4], [0, C]])
                r = v.tensor_tensor(outap, dm, lm, A.mult)
            return r
        op('v', fn, waits=['s_lamtm', 'd_dmatT', 's_dk2'], key='v_wi')

        def fn(s):
            return s.activation(wintraT[:], setup_arg[:, 0:2048], AF.Exp, scale=-1.0)
        op('s', fn, waits=['v_wi'], key='s_wi')

        def csum_build(nm, slot, dst, waits):
            def fn(t, slot=slot):
                r = None
                for kc in range(8):
                    for nh in range(2):
                        r = t.matmul(p_out[0:1, nh * 512:(nh + 1) * 512], onec[:],
                                     slot[:, kc * E + nh * 512:kc * E + (nh + 1) * 512],
                                     start=(kc == 0), stop=(kc == 7))
                return r
            op('t', fn, waits=waits, key='t_cs_' + nm)

            def fn(s, dst=dst):
                return s.copy(dst[:], p_out[0:1, 0:1024])
            op('s', fn, waits=['t_cs_' + nm], key='s_cs_' + nm)
        csum_build('k', wA, csumk, ['d_wk', 's_wi'])
        csum_build('v', wB, csumv, ['d_wv', 's_cs_k'])

        last_fold = ['s_atiles']
        prev_pproj = ['s_cs_v']
        prev_ffold = [None]
        for k in range(s1):
            pi_ = k % 2
            kk = k - NOUR
            ours = k >= NOUR
            ck = f'c{k}_'
            if k > 0 and k % 8 == 0:
                stg = k // 8
                def fn(g, stg=stg):
                    return g.dma_start(out=costab[:, 0:512],
                                       in_=h['cos_tm'][:, stg * 512:(stg + 1) * 512])
                op('gd', fn, waits=[f'c{k-1}_ktc', f'c{k-1}_kts'],
                   key=f'd_cosq{stg}a', dma='const')

                def fn(g, stg=stg):
                    return g.dma_start(out=costab[:, 512:1024],
                                       in_=h['s2_tm'][:, stg * 512:(stg + 1) * 512])
                op('gd', fn, key=f'd_cosq{stg}', dma='const')

            def fn(g, k=k, pi_=pi_):
                src = AP(tensor=h['xT'].tensor, offset=h['xT'].offset + k * C,
                         ap=[[T, C], [C * T, 8], [1, C]])
                return g.dma_start(out=xfm[pi_].rearrange("p (a b) -> p a b", a=8),
                                   in_=src)
            wv = [f'c{k-1}_vproj'] if k >= 1 else []
            op('gd', fn, waits=wv, key=ck + 'dxf', dma='xf')

            def fn(g, k=k):
                return g.dma_start(out=xtmb, in_=h['x'][k * C:(k + 1) * C, :])
            wv = [f'c{k-1}_sx', f'c{k-1}_sx2'] if k >= 1 else []
            op('gd', fn, waits=wv, key=ck + 'dxt', dma='xtb')

            if ours:
                def fn(v, kk=kk):
                    return v.tensor_copy(ssnap[:, kk * 512:(kk + 1) * 512], srun[:])
                op('v', fn, waits=[last_fold[0]], key=ck + 'snap')

            def fn(v):
                return v.reduce_sum(accv[:], xtmb, axis=mybir.AxisListType.X)
            op('v', fn, waits=[ck + 'dxt'], key=ck + 'sx')

            def fn(s):
                return s.activation(sq_dump, xtmb, AF.Square, accum_out=accv2[:])
            wv = [ck + 'dxt'] + ([f'c{k-1}_ktc'] if k >= 1 else [])
            op('s', fn, waits=wv, key=ck + 'sx2')

            def fn(v):
                v.tensor_scalar(tiny[:, 0:1], accv[:], -1.0 / E, None, A.mult)
                v.tensor_scalar(tiny[:, 1:2], accv2[:], 1.0 / E, None, A.mult)
                v.tensor_tensor(tiny[:, 2:3], tiny[:, 0:1], tiny[:, 0:1], A.mult)
                return v.tensor_tensor(tiny[:, 3:4], tiny[:, 1:2], tiny[:, 2:3],
                                       A.subtract)
            wv = [ck + 'sx2'] + ([f'c{k-1}_st2'] if k >= 1 else [])
            op('v', fn, waits=wv, key=ck + 'st1')

            def fn(s):
                s.activation(tiny[:, 4:5], tiny[:, 3:4], AF.Ln, bias=epsb[:])
                s.activation(tiny[:, 5:6], tiny[:, 4:5], AF.Exp, scale=-0.5)
                return s.activation(tiny[:, 6:7], tiny[:, 4:5], AF.Exp, scale=0.5)
            op('s', fn, waits=[ck + 'st1'], key=ck + 'st2')

            def fn(v, k=k):
                v.tensor_copy(stats[:, k * 3:k * 3 + 1], tiny[:, 0:1])
                v.tensor_copy(stats[:, k * 3 + 1:k * 3 + 2], tiny[:, 6:7])
                v.tensor_copy(stats[:, k * 3 + 2:k * 3 + 3], tiny[:, 5:6])
                return v.tensor_copy(statbf[:, k * 3:k * 3 + 3],
                                     stats[:, k * 3:k * 3 + 3])
            op('v', fn, waits=[ck + 'st2'], key=ck + 'st3')

            def fn(t, k=k):
                r = None
                for i3 in range(3):
                    r = t.matmul(p_out[0:1, i3 * C:(i3 + 1) * C],
                                 statbf[:, k * 3 + i3:k * 3 + i3 + 1],
                                 ident[:], start=(i3 == 0), stop=(i3 == 2))
                return r
            wv = [ck + 'st3', 'd_ident'] + \
                ([f'c{k-1}_stev', f'c{k-1}_kdec'] if k else ['s_cs_v'])
            op('t', fn, waits=wv, key=ck + 'strow')

            def fn(s):
                return s.copy(statT[0:1, :], p_out[0:1, 0:3 * C])
            op('s', fn, waits=[ck + 'strow'], key=ck + 'stev')

            def kvproj(t, slot, crow, brow, pw):
                r = None
                for kc in range(8):
                    for nh in range(2):
                        t.matmul(p_proj[:, nh * 512:(nh + 1) * 512],
                                 xfm[pw][:, kc * C:(kc + 1) * C],
                                 slot[:, kc * E + nh * 512:kc * E + (nh + 1) * 512],
                                 start=(kc == 0), stop=False)
                for nh in range(2):
                    t.matmul(p_proj[:, nh * 512:(nh + 1) * 512], statT[0:1, 0:C],
                             crow[0:1, nh * 512:(nh + 1) * 512], start=False,
                             stop=False)
                    r = t.matmul(p_proj[:, nh * 512:(nh + 1) * 512], statT[0:1, C:2 * C],
                                 brow[0:1, nh * 512:(nh + 1) * 512], start=False,
                                 stop=True)
                return r
            op('t', lambda t, pw=pi_: kvproj(t, wA, csumk, browk, pw),
               waits=[ck + 'dxf', ck + 'stev', 'd_wk', 's_cs_k', prev_pproj[0]],
               key=ck + 'kproj')

            def fn(s, k=k):
                a = p_proj[:].rearrange("p (a b) -> p a b", b=2)
                o = kshuf.rearrange("p (a b) -> p a b", b=2)
                s.mul(o[:, :, 0], a[:, :, 1], stats[:, k * 3 + 2:k * 3 + 3])
                return s.mul(o[:, :, 1], a[:, :, 0], stats[:, k * 3 + 2:k * 3 + 3])
            op('s', fn, waits=[ck + 'kproj'], key=ck + 'kev')

            def fn(v, k=k):
                cosap = zs_free(costab[:], (k % 8) * D, H, D)
                return v.tensor_tensor(p_out[:], p_proj[:], cosap, A.mult)
            wv = [ck + 'kev', f'd_cosq{k // 8}', ck + 'stev']
            op('v', fn, waits=wv, key=ck + 'ktc')

            def fn(g, k=k):
                s2ap = zs_free(costab[:], 512 + (k % 8) * D, H, D)
                return g.tensor_tensor(tc2, kshuf, s2ap, A.mult)
            op('g', fn, waits=[ck + 'kev', f'd_cosq{k // 8}'],
               key=ck + 'kts')

            krot_dst = krot_st[:, (k % 16) * E:((k % 16) + 1) * E]

            def fn(v, krot_dst=krot_dst, k=k):
                return v.scalar_tensor_tensor(krot_dst, p_out[:],
                                              stats[:, k * 3 + 2:k * 3 + 3],
                                              tc2, A.mult, A.add)
            op('v', fn, waits=[ck + 'kts', ck + 'ktc'], key=ck + 'krot')

            vari = 2 if ours else (k % 2)

            def fn(v, vari=vari, krot_dst=krot_dst):
                return v.tensor_tensor(kdec, krot_dst,
                                       ovl1[:, vari * E:(vari + 1) * E], A.mult)
            op('v', fn, waits=[ck + 'krot', f's_dk{vari}'], key=ck + 'kdec')

            if not ours:
                def fn(g, krot_dst=krot_dst):
                    return g.tensor_scalar(kmsk, krot_dst, fmcol[:], None, A.mult)
                op('g', fn, waits=[ck + 'krot', 'd_fmask'], key=ck + 'kmsk')

            op('t', lambda t, pw=pi_: kvproj(t, wB, csumv, browv, pw),
               waits=[ck + 'krot', 'd_wv', 's_cs_v'], key=ck + 'vproj')

            vdst = v_st[:, (k % 16) * E:((k % 16) + 1) * E]

            def fn(s, k=k, vdst=vdst):
                return s.mul(vdst, p_proj[:], stats[:, k * 3 + 2:k * 3 + 3])
            op('s', fn, waits=[ck + 'vproj'], key=ck + 'vev')
            prev_pproj[0] = ck + 'vev'

            win_first = ours or (k % 2 == 0)
            win_last = ours or (k % 2 == 1)

            def fn(t, vdst=vdst, win_first=win_first, win_last=win_last):
                r = None
                for pr in range(8):
                    r = t.matmul(p_sh[:, pr * C:(pr + 1) * C],
                                 kdec[:, pr * C:(pr + 1) * C],
                                 vdst[:, pr * C:(pr + 1) * C],
                                 start=(win_first and pr in (0, 4)),
                                 stop=(win_last and pr in (3, 7)))
                return r
            gw = [ck + 'kdec', ck + 'vev']
            if win_first:
                gw.append(last_fold[0])
            op('t', fn, waits=gw, key=ck + 'gmm')

            if not ours:
                def fn(t, vdst=vdst):
                    r = None
                    for pr in range(8):
                        r = t.matmul(p_F[:, pr * C:(pr + 1) * C],
                                     kmsk[:, pr * C:(pr + 1) * C],
                                     vdst[:, pr * C:(pr + 1) * C],
                                     start=(pr in (0, 4)), stop=(pr in (3, 7)))
                    return r
                fw = [ck + 'kmsk', ck + 'vev']
                if prev_ffold[0]:
                    fw.append(prev_ffold[0])
                op('t', fn, waits=fw, key=ck + 'fmm')

                def fn(v):
                    pfb3 = p_F[:]
                    tl = AP(tensor=pfb3.tensor, offset=pfb3.offset,
                            ap=[[1024, 64], [C, 8], [1, 64]])
                    br = AP(tensor=pfb3.tensor, offset=pfb3.offset + 64 * 1024 + 64,
                            ap=[[1024, 64], [C, 8], [1, 64]])
                    v.tensor_tensor(fsb[0:64, :].rearrange("p (a b) -> p a b", a=8),
                                    fsb[0:64, :].rearrange("p (a b) -> p a b", a=8),
                                    tl, A.add)
                    return v.tensor_tensor(
                        fsb[64:128, :].rearrange("p (a b) -> p a b", a=8),
                        fsb[64:128, :].rearrange("p (a b) -> p a b", a=8),
                        br, A.add)
                op('v', fn, waits=[ck + 'fmm'], key=ck + 'ffold')
                prev_ffold[0] = ck + 'ffold'

            if ours or (k % 2 == 1):
                atile = a128 if ours else a256

                def fn(v, atile=atile):
                    v.tensor_tensor(fold_t, srun[:], atile[:], A.mult)
                    shb = p_sh[:]
                    tl = AP(tensor=shb.tensor, offset=shb.offset,
                            ap=[[1024, 64], [C, 8], [1, 64]])
                    br = AP(tensor=shb.tensor, offset=shb.offset + 64 * 1024 + 64,
                            ap=[[1024, 64], [C, 8], [1, 64]])
                    v.tensor_tensor(srun[0:64, :].rearrange("p (a b) -> p a b", a=8),
                                    fold_t[0:64, :].rearrange("p (a b) -> p a b", a=8),
                                    tl, A.add)
                    return v.tensor_tensor(
                        srun[64:128, :].rearrange("p (a b) -> p a b", a=8),
                        fold_t[64:128, :].rearrange("p (a b) -> p a b", a=8),
                        br, A.add)
                wv = [ck + 'gmm'] + ([ck + 'snap'] if ours else [])
                op('v', fn, waits=wv, key=ck + 'fold')
                last_fold[0] = ck + 'fold'

        if not s2:
            def fn(v):
                return v.tensor_copy(x1, srun[:].rearrange("p (a b) -> p (a b)") if False else srun[:])
            def fn2(v):
                v.tensor_copy(x1[:, 0:512], srun[:])
                return v.tensor_copy(x1[:, 512:1024], fsb[:])
            op('v', fn2, waits=[last_fold[0], prev_ffold[0]], key='dbg_cp')
            def fn3(sy):
                return sy.dma_start(out=outh[0:C, :], in_=x1)
            op('sy', fn3, waits=['dbg_cp'], key='dbg_ost', osem=True)

        if s2:
            op('gd', lambda g: wload(g, wC, h['wq_t']), key='d_wq', dma='wC')
            op('gd', lambda g: wload(g, wA, h['wg_t']), waits=['c31_kproj'],
               key='d_wg', dma='wA')
            if not s2:
                plan['sy'].append({'fn': lambda sy: None,
                                   'waits': [('o', cnt['o'])], 'inc': None})
            csum_build('q', wC, csumq, ['d_wq', 'c31_kproj'])
            csum_build('g', wA, csumg, ['d_wg', 'c31_vproj', 's_cs_q'])

            def fn(g):
                return g.dma_start(out=browk[:], in_=h['bq_r'])
            op('gd', fn, waits=['c31_kproj'], key='v_bq2', dma='rowq')

            def fn(g):
                return g.dma_start(out=browv[:], in_=h['bg_r'])
            op('gd', fn, waits=['c31_vproj'], key='v_bg2', dma='rowg')

            for sl in range(8):
                def fn(v, sl=sl):
                    return v.tensor_scalar(uc[:, 0:128], iota_t[:],
                                           lamfmc[:, sl:sl + 1], None, A.mult)
                wv = ['c31_kdec', 'd_iota_t'] + ([f's_dq{sl-1}'] if sl else [])
                op('v', fn, waits=wv, key=f'v_dq{sl}')

                def fn(s, sl=sl):
                    return s.activation(ovl1[:, sl * C:(sl + 1) * C],
                                        uc[:, 0:128], AF.Exp, scale=-1.0)
                op('s', fn, waits=[f'v_dq{sl}'], key=f's_dq{sl}')
            decqfm = ovl1[:, 0:1024]
            bffT = ovl1[:, 1024:2048]

            def fn(g):
                return g.dma_start(out=tc2[0:1, :], in_=h['bf_r'])
            op('gd', fn, waits=['c31_krot'], key='d_bf', dma='bfrow')

            def fn(t):
                t.matmul(p_out[:, 0:512], ones1[:], tc2[0:1, 0:512], start=True, stop=True)
                return t.matmul(p_out[:, 512:1024], ones1[:], tc2[0:1, 512:1024],
                                start=True, stop=True)
            op('t', fn, waits=['d_bf', 's_cs_g'], key='t_bff')

            def fn(s):
                return s.copy(bffT, p_out[:])
            op('s', fn, waits=['t_bff', 's_dq7'], key='s_bff')

            def fn(g):
                return g.dma_start(out=cfm32[:], in_=h['cos_fm'][:, 1024:2048])
            op('gd', fn, waits=['c31_kdec'], key='d_cfm', dma='cfm')

            def fn(g):
                return g.dma_start(out=costab[:], in_=h['s2_fm'][:, 1024:2048])
            op('gd', fn, waits=['c31_ktc', 'c31_kts'], key='d_sfm', dma='const')
            cosfm = cfm32
            s2fm = costab

            for wc in range(8):
                def fn(g, wc=wc):
                    src = AP(tensor=h['wf_t'].tensor,
                             offset=h['wf_t'].offset + wc * C * E,
                             ap=[[E, C], [1, E]])
                    return g.dma_start(out=wsc_f, in_=src)
                wv = ['c31_vev'] if wc == 0 else [f'wq{wc-1}b']
                op('gd', fn, waits=wv, key=f'd_wf{wc}', dma='wf')
                for hh in range(2):
                    vsrc = wsc_f[:, hh * 512:(hh + 1) * 512]
                    vb = vsrc.bitcast(I32)
                    dst = wB[:, wc * E + hh * 512:wc * E + (hh + 1) * 512]

                    def fn(e, vb=vb, vsrc=vsrc, dst=dst):
                        e.tensor_scalar(ei_i[:], vb, 23, 255, A.logical_shift_right,
                                        A.bitwise_and)
                        e.tensor_scalar(e2_i[:], ei_i[:], 121, 135, A.max, A.min)
                        e.tensor_scalar(t3_i[:], e2_i[:], 3, None, A.subtract)
                        e.tensor_scalar(t3_i[:], t3_i[:], 23, None, A.logical_shift_left)
                        e.tensor_scalar(t4_i[:], t3_i[:], -1, 2130706432, A.mult, A.add)
                        e.tensor_tensor(uc, vsrc, t4_i[:].bitcast(F32), A.mult)
                        e.tensor_scalar(qo_i[:], uc, 1.0, None, A.mult)
                        e.tensor_copy(u2, qo_i[:])
                        e.tensor_tensor(u2, u2, t3_i[:].bitcast(F32), A.mult)
                        e.tensor_scalar(u2, u2, 448.0, -448.0, A.min, A.max)
                        return e.tensor_copy(dst, u2)
                    wv = [f'd_wf{wc}', 'c31_fold' if wc == 0 else f'wq{wc-1}b']
                    if wc == 0:
                        wv.append('v_bq2')
                        wv.append('s_dq7')
                    op('v', fn, waits=wv, key=f'wq{wc}' + 'ab'[hh])

            prev = {'pproj': 'c31_vev', 'psh': 'c31_fold', 'pout': 's_bff',
                    'fupd': prev_ffold[0], 'xf': 'd_sfm', 'ost': None,
                    'dxf': 'c31_vproj', 'quant': 'wq7b', 'x1w': None, 'qx1': None}
            s2cnt = [0]

            def op2(*a, **kw):
                s2cnt[0] += 1
                if s2ops is not None and s2cnt[0] > s2ops:
                    return
                return op(*a, **kw)

            for k in range(NCH - 1, NCH - 1 - s2n, -1):
                kk = k - NOUR
                pi_ = k % 2
                ck = f'u{k}_'
                if k == 23:
                    def fn(g):
                        return g.dma_start(out=cfm32[:], in_=h['cos_fm'][:, 0:1024])
                    op2('gd', fn, waits=['u24_qev'], key='d_cfm1', dma='cfm')

                    def fn(g):
                        return g.dma_start(out=costab[:], in_=h['s2_fm'][:, 0:1024])
                    op2('gd', fn, waits=['u24_qrot'], key='d_sfm1', dma='const')

                def fn(sy, k=k):
                    return sy.dma_start(out=xtmf, in_=h['x'][k * C:(k + 1) * C, :])
                op2('sy', fn, waits=[prev['xf']], key=ck + 'dxt', dma='xtf')

                def fn(g, k=k, pi_=pi_):
                    src = AP(tensor=h['xT'].tensor, offset=h['xT'].offset + k * C,
                             ap=[[T, C], [C * T, 8], [1, C]])
                    return g.dma_start(out=xfm[pi_].rearrange("p (a b) -> p a b", a=8),
                                       in_=src)
                op2('gd', fn, waits=[prev['dxf']], key=ck + 'dxf', dma='xf')

                def fn(t, k=k):
                    r = None
                    for i3 in range(3):
                        r = t.matmul(p_sh[0:1, i3 * C:(i3 + 1) * C],
                                     statbf[:, k * 3 + i3:k * 3 + i3 + 1],
                                     ident[:], start=(i3 == 0), stop=(i3 == 2))
                    return r
                op2('t', fn, waits=[prev['psh'], 'd_cfm', 'd_sfm'], key=ck + 'strow')

                def fn(s):
                    return s.copy(statT[0:1, :], p_sh[0:1, 0:3 * C])
                op2('s', fn, waits=[ck + 'strow'], key=ck + 'stev')

                def fn(t, pw=pi_):
                    r = None
                    for sl in range(8):
                        dst = p_proj[:, sl * C:(sl + 1) * C]
                        for kc in range(8):
                            t.matmul(dst, wC[:, kc * E + sl * C:kc * E + (sl + 1) * C],
                                     xfm[pw][:, kc * C:(kc + 1) * C],
                                     start=(kc == 0 and sl in (0, 4)), stop=False)
                        t.matmul(dst, csumq[0:1, sl * C:(sl + 1) * C],
                                 statT[0:1, 0:C], start=False, stop=False)
                        r = t.matmul(dst, browq[0:1, sl * C:(sl + 1) * C],
                                     statT[0:1, C:2 * C], start=False,
                                     stop=(sl in (3, 7)))
                    return r
                op2('t', fn, waits=[ck + 'dxf', ck + 'stev', prev['pproj'],
                                   's_cs_q', 'v_bq2'], key=ck + 'qproj')

                def fn(v, kk=kk):
                    v.tensor_copy(qfm_raw, p_proj[:])
                    cfm = zs_free(cosfm[:], (kk % 8) * C, 8, C)
                    return v.tensor_tensor(p_out[:], p_proj[:], cfm, A.mult)
                op2('v', fn, waits=[ck + 'qproj', prev['pout'], 's_bff',
                                   'd_cfm' if kk >= 8 else 'd_cfm1'],
                   key=ck + 'qev')

                def fn(t):
                    r = None
                    for sl in range(8):
                        r = t.matmul(p_proj[:, sl * C:(sl + 1) * C], pswap[:],
                                     qfm_raw[:, sl * C:(sl + 1) * C],
                                     start=(sl in (0, 4)), stop=(sl in (3, 7)))
                    return r
                op2('t', fn, waits=[ck + 'qev', 'd_pswap'], key=ck + 'qswap')

                def fn(v, kk=kk):
                    sfm = zs_free(s2fm[:], (kk % 8) * C, 8, C)
                    v.tensor_tensor(tc2, p_proj[:], sfm, A.mult)
                    v.tensor_tensor(qrot, p_out[:], tc2, A.add)
                    v.tensor_tensor(qlo, p_out[:], qrot, A.subtract)
                    v.tensor_tensor(qlo, qlo, tc2, A.add)
                    return v.tensor_tensor(qdfm, qrot, decqfm, A.mult)
                op2('v', fn, waits=[ck + 'qswap',
                                   'd_sfm' if kk >= 8 else 'd_sfm1'],
                   key=ck + 'qrot')

                def fn(t, kk=kk):
                    r = None
                    for sl in range(8):
                        r = t.matmul(p_sh[:, sl * C:(sl + 1) * C],
                                     krot_st[:, kk * E + sl * C:kk * E + (sl + 1) * C],
                                     ident[:], start=(sl in (0, 4)),
                                     stop=(sl in (3, 7)))
                    return r
                op2('t', fn, waits=[ck + 'stev', ck + 'qrot'], key=ck + 'kfmm')

                def fn(s):
                    return s.copy(kfm, p_sh[:])
                op2('s', fn, waits=[ck + 'kfmm'], key=ck + 'kfme')

                def fn(t, pw=pi_):
                    r = None
                    for kc in range(8):
                        for nh in range(2):
                            t.matmul(p_proj[:, nh * 512:(nh + 1) * 512],
                                     xfm[pw][:, kc * C:(kc + 1) * C],
                                     wA[:, kc * E + nh * 512:kc * E + (nh + 1) * 512],
                                     start=(kc == 0), stop=False)
                    for nh in range(2):
                        t.matmul(p_proj[:, nh * 512:(nh + 1) * 512], statT[0:1, 0:C],
                                 csumg[0:1, nh * 512:(nh + 1) * 512], start=False,
                                 stop=False)
                        r = t.matmul(p_proj[:, nh * 512:(nh + 1) * 512],
                                     statT[0:1, C:2 * C],
                                     browg[0:1, nh * 512:(nh + 1) * 512], start=False,
                                     stop=True)
                    return r
                op2('t', fn, waits=[ck + 'qrot', 'd_wg', 's_cs_g', 'v_bg2'],
                   key=ck + 'gproj')

                def fn(s, k=k):
                    s.mul(x1, p_proj[:], stats[:, k * 3 + 2:k * 3 + 3])
                    s.activation(uc, x1[:, 0:512], AF.Tanh, scale=0.5)
                    return s.activation(u2, x1[:, 512:1024], AF.Tanh, scale=0.5)
                op2('s', fn, waits=[ck + 'gproj', prev['x1w']] if prev['x1w'] else
                   [ck + 'gproj'], key=ck + 'gev')

                def fn(v):
                    v.tensor_scalar(uc, uc, 0.5, 0.5, A.mult, A.add)
                    return v.tensor_scalar(u2, u2, 0.5, 0.5, A.mult, A.add)
                op2('v', fn, waits=[ck + 'gev'], key=ck + 'gate')

                for w in range(2):
                    def fn(t, w=w):
                        r = None
                        for pr in range(4):
                            p2 = w * 4 + pr
                            t.matmul(p_sh[:, pr * C:(pr + 1) * C],
                                     kfm[0:64, p2 * C:(p2 + 1) * C],
                                     qrot[0:64, p2 * C:(p2 + 1) * C],
                                     start=(pr == 0), stop=(pr == 3),
                                     tile_position=(0, 0))
                            r = t.matmul(p_sh[:, 512 + pr * C:512 + (pr + 1) * C],
                                         kfm[64:128, p2 * C:(p2 + 1) * C],
                                         qrot[64:128, p2 * C:(p2 + 1) * C],
                                         start=(pr == 0), stop=(pr == 3),
                                         tile_position=(64, 0))
                        return r
                    wt = [ck + 'kfme', ck + 'qrot']
                    if w == 1:
                        wt.append(ck + 'ptw0')
                    op2('t', fn, waits=wt, key=ck + f'ptmm{w}')

                    def fn(v, w=w):
                        return v.tensor_tensor(ptw[:], p_sh[:],
                                               wintraT[:, w * 1024:(w + 1) * 1024],
                                               A.mult)
                    wvv = [ck + f'ptmm{w}', 's_wi']
                    if w == 1:
                        wvv.append(ck + 'diag0')
                    op2('v', fn, waits=wvv, key=ck + f'ptw{w}')

                    def fn(t, kk=kk, w=w):
                        r = None
                        for c2 in range(8):
                            h2 = w * 8 + (2 * c2 if c2 < 4 else 2 * (c2 - 4) + 1)
                            pcol = (h2 % 2) * 512 + (h2 // 2) * D
                            r = t.matmul(p_out[:, pcol:pcol + D],
                                         ptw[:, c2 * C:(c2 + 1) * C],
                                         v_st[:, kk * E + h2 * D:kk * E + (h2 + 1) * D],
                                         start=(w == 0 and c2 in (0, 4)), stop=False)
                        return r
                    wt = [ck + f'ptw{w}']
                    if w == 0:
                        wt.append(ck + 'qrot')
                    op2('t', fn, waits=wt, key=ck + f'diag{w}')
                if ck + 'diag1' in marks:
                    marks[ck + 'diag'] = marks[ck + 'diag1']

                def fn(s):
                    return s.copy(fev, fsb[:])
                fw = [prev['fupd'], ck + 'kfme']
                if prev['qx1']:
                    fw.append(prev['qx1'])
                op2('s', fn, waits=fw, key=ck + 'fev')

                def fn(v):
                    return v.tensor_tensor(fev_lo, fsb[:], fev, A.subtract)
                op2('v', fn, waits=[ck + 'fev'], key=ck + 'fevlo')

                def fn(t, kk=kk):
                    r = None
                    for pr in range(8):
                        for hf in range(2):
                            pcol = hf * 512 + pr * D
                            t.matmul(p_out[:, pcol:pcol + D],
                                     qdfm[hf * 64:(hf + 1) * 64, pr * C:(pr + 1) * C],
                                     ssnap[hf * 64:(hf + 1) * 64,
                                           kk * 512 + pr * 64:kk * 512 + (pr + 1) * 64],
                                     start=False, stop=False,
                                     tile_position=(hf * 64, 0))
                    for qv, fv, last in ((qrot, fev, False), (qrot, fev_lo, False),
                                         (qlo, fev, True)):
                        for pr in range(8):
                            for hf in range(2):
                                pcol = hf * 512 + pr * D
                                r = t.matmul(
                                    p_out[:, pcol:pcol + D],
                                    qv[hf * 64:(hf + 1) * 64, pr * C:(pr + 1) * C],
                                    fv[hf * 64:(hf + 1) * 64, pr * 64:(pr + 1) * 64],
                                    start=False,
                                    stop=(last and pr == 7),
                                    tile_position=(hf * 64, 0))
                    return r
                op2('t', fn, waits=[ck + 'diag1', ck + 'fevlo'], key=ck + 'apply')

                def fn(t, kk=kk):
                    r = None
                    for pr in range(8):
                        r = t.matmul(p_F[:, pr * C:(pr + 1) * C],
                                     krot_st[:, kk * E + pr * C:kk * E + (pr + 1) * C],
                                     v_st[:, kk * E + pr * C:kk * E + (pr + 1) * C],
                                     start=(pr in (0, 4)), stop=(pr in (3, 7)))
                    return r
                op2('t', fn, waits=[ck + 'fev'], key=ck + 'fmm2')

                def fn(v):
                    pfb3 = p_F[:]
                    tl = AP(tensor=pfb3.tensor, offset=pfb3.offset,
                            ap=[[1024, 64], [C, 8], [1, 64]])
                    br = AP(tensor=pfb3.tensor, offset=pfb3.offset + 64 * 1024 + 64,
                            ap=[[1024, 64], [C, 8], [1, 64]])
                    v.tensor_tensor(fsb[0:64, :].rearrange("p (a b) -> p a b", a=8),
                                    fsb[0:64, :].rearrange("p (a b) -> p a b", a=8),
                                    tl, A.add)
                    return v.tensor_tensor(
                        fsb[64:128, :].rearrange("p (a b) -> p a b", a=8),
                        fsb[64:128, :].rearrange("p (a b) -> p a b", a=8),
                        br, A.add)
                op2('v', fn, waits=[ck + 'fmm2'], key=ck + 'fupd')
                prev['fupd'] = ck + 'fupd'

                def fn(v, k=k):
                    pob = p_out[:]
                    poperm = AP(tensor=pob.tensor, offset=pob.offset,
                                ap=[list(pob.ap[0]), [D, 8], [512, 2], [1, D]])
                    v.tensor_tensor(x1.rearrange("p (a b c) -> p a b c", a=8, b=2),
                                    x1.rearrange("p (a b c) -> p a b c", a=8, b=2),
                                    poperm, A.mult)
                    v.tensor_scalar(x1, x1, stats[:, k * 3 + 2:k * 3 + 3], None, A.mult)
                    v.tensor_tensor(x1[:, 0:512], x1[:, 0:512], uc, A.mult)
                    v.tensor_tensor(x1[:, 512:1024], x1[:, 512:1024], u2, A.mult)
                    return v.tensor_tensor(x1, x1, xtmf, A.add)
                wv = [ck + 'apply', ck + 'gate', ck + 'dxt']
                if prev['ost']:
                    wv.append(prev['ost'])
                op2('v', fn, waits=wv, key=ck + 'x1')
                prev['pout'] = ck + 'x1'

                def fn(v):
                    return v.reduce_sum(accv[:], x1, axis=mybir.AxisListType.X)
                op2('v', fn, key=ck + 'l2a')

                def fn(s):
                    return s.activation(ffnq, x1, AF.Square, accum_out=accv2[:])
                op2('s', fn, waits=[ck + 'x1'] +
                   ([prev['quant']] if prev['quant'] else []), key=ck + 'l2b')

                def fn(v):
                    v.tensor_scalar(tiny[:, 0:1], accv[:], -1.0 / E, None, A.mult)
                    v.tensor_scalar(tiny[:, 1:2], accv2[:], 1.0 / E, None, A.mult)
                    v.tensor_tensor(tiny[:, 2:3], tiny[:, 0:1], tiny[:, 0:1], A.mult)
                    return v.tensor_tensor(tiny[:, 3:4], tiny[:, 1:2], tiny[:, 2:3],
                                           A.subtract)
                op2('v', fn, waits=[ck + 'l2b'], key=ck + 'l2c')

                def fn(s):
                    s.activation(tiny[:, 4:5], tiny[:, 3:4], AF.Ln, bias=epsb[:])
                    return s.activation(tiny[:, 5:6], tiny[:, 4:5], AF.Exp, scale=-0.5)
                op2('s', fn, waits=[ck + 'l2c'], key=ck + 'l2d')

                def fn(v):
                    return v.tensor_scalar(vq, x1, tiny[:, 0:1], tiny[:, 5:6],
                                           A.add, A.mult)
                op2('v', fn, waits=[ck + 'l2d'], key=ck + 'vq')

                for hh in range(2):
                    vsrc = vq[:, hh * 512:(hh + 1) * 512]
                    vb = vsrc.bitcast(I32)
                    dst = qxb[:, hh * 512:(hh + 1) * 512]

                    def fn(e, vb=vb, dst=dst):
                        e.tensor_scalar(ei_i[:], vb, 23, 255, A.logical_shift_right,
                                        A.bitwise_and)
                        e.tensor_scalar(e2_i[:], ei_i[:], 121, 135, A.max, A.min)
                        e.tensor_scalar(t3_i[:], e2_i[:], 3, None, A.subtract)
                        e.tensor_scalar(t3_i[:], t3_i[:], 23, None,
                                        A.logical_shift_left)
                        e.tensor_scalar(t4_i[:], t3_i[:], -1, 2130706432, A.mult,
                                        A.add)
                        e.tensor_scalar(e2_i[:], vb, -2147483648, None, A.bitwise_and)
                        e.tensor_tensor(t3_i[:], t3_i[:], e2_i[:], A.bitwise_or)
                        e.tensor_scalar(ei_i[:], vb, 2147483647, None, A.bitwise_and)
                        e.tensor_tensor(uc, ei_i[:].bitcast(F32),
                                        t4_i[:].bitcast(F32), A.mult)
                        e.tensor_scalar(uc, uc, -7.5, None, A.add)
                        for lev in [8.0, 4.0, 2.0, 1.0]:
                            e.tensor_scalar(bq_t, uc, lev, None, A.is_ge)
                            e.scalar_tensor_tensor(uc, bq_t, -lev, uc, A.mult, A.add)
                        e.tensor_tensor(u2, ei_i[:].bitcast(F32),
                                        t4_i[:].bitcast(F32), A.mult)
                        e.tensor_scalar(u2, u2, -7.5, None, A.add)
                        e.tensor_tensor(u2, u2, uc, A.subtract)
                        e.scalar_tensor_tensor(u2, u2, 8.0, t3_i[:].bitcast(F32),
                                               A.add, A.mult)
                        e.tensor_scalar(u2, u2, 448.0, -448.0, A.min, A.max)
                        return e.tensor_copy(dst, u2)
                    wv = [ck + 'vq'] + ([ck + 'qx0'] if hh else [])
                    op2('v', fn, waits=wv, key=ck + f'qx{hh}')

                def fn(t):
                    r = None
                    for sl in range(8):
                        r = t.matmul(p_sh[:, sl * C:(sl + 1) * C],
                                     qxb[:, sl * C:(sl + 1) * C], ident[:],
                                     start=(sl in (0, 4)), stop=(sl in (3, 7)))
                    return r
                op2('t', fn, waits=[ck + 'qx1', ck + 'ptw1'], key=ck + 'qxT')

                def fn(s):
                    return s.copy(graw, p_sh[:])
                op2('s', fn, waits=[ck + 'qxT', ck + 'gate'], key=ck + 'qxfe')
                prev['psh'] = ck + 'qxfe'
                qxfm = graw

                def fn(t, qxfm=qxfm):
                    r = None
                    for kc in range(8):
                        for nh in range(2):
                            r = t.matmul(p_proj[:, nh * 512:(nh + 1) * 512],
                                         qxfm[:, kc * C:(kc + 1) * C],
                                         wB[:, kc * E + nh * 512:kc * E + (nh + 1) * 512],
                                         start=(kc == 0), stop=(kc == 7))
                    return r
                op2('t', fn, waits=[ck + 'qxfe', ck + 'gev', 'wq7b'], key=ck + 'ffn')

                for hh in range(2):
                    psrc = p_proj[:, hh * 512:(hh + 1) * 512]
                    vb = psrc.bitcast(I32)
                    dst = ffnq[:, hh * 512:(hh + 1) * 512]

                    def fn(e, vb=vb, psrc=psrc, dst=dst):
                        e.tensor_scalar(ei_i[:], vb, 23, 255, A.logical_shift_right,
                                        A.bitwise_and)
                        e.tensor_scalar(e2_i[:], ei_i[:], 113, 142, A.max, A.min)
                        e.tensor_scalar(t3_i[:], e2_i[:], 10, None, A.subtract)
                        e.tensor_scalar(t3_i[:], t3_i[:], 23, None,
                                        A.logical_shift_left)
                        e.tensor_scalar(t4_i[:], t3_i[:], -1, 2130706432, A.mult,
                                        A.add)
                        e.tensor_tensor(uc, psrc, t4_i[:].bitcast(F32), A.mult)
                        e.tensor_scalar(qo_i[:], uc, 1.0, None, A.mult)
                        e.tensor_copy(u2, qo_i[:])
                        e.tensor_tensor(u2, u2, t3_i[:].bitcast(F32), A.mult)
                        return e.tensor_copy(dst, u2)
                    wv = [ck + 'ffn'] + ([ck + 'fr0'] if hh else [ck + 'qx1'])
                    op2('v', fn, waits=wv, key=ck + f'fr{hh}')
                prev['pproj'] = ck + 'fr1'
                prev['quant'] = ck + 'fr1'

                def fn(g):
                    return g.tensor_tensor(xtmf, x1, bffT, A.add)
                op2('g', fn, waits=[ck + 'l2b', 's_bff', ck + 'qx1'], key=ck + 'x1b')

                def fn(v):
                    if dbg_x1:
                        return v.tensor_scalar(x1, x1, 1.0, None, A.mult)
                    return v.tensor_tensor(x1, xtmf, ffnq, A.add)
                op2('v', fn, waits=[ck + 'x1b', ck + 'fr1'], key=ck + 'fin')
                prev['xf'] = ck + 'fin'

                def fn(sy, kk=kk):
                    return sy.dma_start(out=outh[kk * C:(kk + 1) * C, :], in_=x1)
                op2('sy', fn, waits=[ck + 'fin'], key=ck + 'ost', osem=True)
                prev['ost'] = ck + 'ost'
                prev['x1w'] = ck + 'ost'
                prev['qx1'] = ck + 'qx1'
                prev['dxf'] = ck + 'gproj'

            plan['sy'].append({'fn': lambda sy: None,
                               'waits': [('o', cnt['o'])], 'inc': None})


        state = {k: 0 for k in cnt}
        pos = {k: 0 for k in plan}
        progress = True
        while progress:
            progress = False
            for engk, items in plan.items():
                while pos[engk] < len(items):
                    it = items[pos[engk]]
                    if all(state[sk] >= val for (sk, val) in it['waits']):
                        if it['inc'] is not None:
                            state[it['inc'][0]] += it['inc'][1]
                        pos[engk] += 1
                        progress = True
                    else:
                        break
        stuck = {k: (pos[k], len(plan[k])) for k in plan if pos[k] < len(plan[k])}
        if stuck:
            detail = {}
            for k2, (p2, n2) in stuck.items():
                it = plan[k2][p2]
                detail[k2] = [(sk, val, state[sk]) for (sk, val) in it['waits']
                              if state[sk] < val]
            raise RuntimeError(f"plan deadlock: stuck={stuck} blocking={detail}")

        block = es.enter_context(nc.Block())

        _DRAIN_AFTER = {'tensor_scalar', 'tensor_tensor', 'tensor_copy',
                        'scalar_tensor_tensor', 'reduce_sum', 'memset',
                        'activation', 'mul', 'copy', 'tensor_sub',
                        'tensor_add', 'tensor_mul', 'tensor_scalar_mul'}

        class _AutoDrain:
            def __init__(self, eng):
                self._e = eng

            def __getattr__(self, nm):
                a = getattr(self._e, nm)
                if nm in _DRAIN_AFTER:
                    def w(*args, **kw):
                        r = a(*args, **kw)
                        self._e.drain()
                        return r
                    return w
                return a

        def emit(items, drain=False):
            def body(eng):
                we = _AutoDrain(eng) if drain else eng
                for it in items:
                    for (semk, val) in it['waits']:
                        eng.wait_ge(sems[semk], val)
                    inst = it['fn'](we)
                    if it['inc'] is not None:
                        semk, n = it['inc']
                        inst.then_inc(sems[semk], n)
            return body
        block.tensor(emit(plan['t']))
        block.vector(emit(plan['v'], drain=True))
        block.scalar(emit(plan['s'], drain=True))
        block.gpsimd(emit(plan['g'], drain=True))
        block.sync(emit(plan['sy']))
    return nc


def kernel(L0_attn_x, L0_attn_Wq, L0_attn_bq, L0_attn_Wk, L0_attn_bk,
           L0_attn_Wv, L0_attn_bv, L0_attn_Wg, L0_attn_bg, L0_attn_gamma,
           L0_ffn_weight, L0_ffn_bias):
    x = np.asarray(L0_attn_x)
    W = [np.asarray(w) for w in [L0_attn_Wq, L0_attn_Wk, L0_attn_Wv, L0_attn_Wg]]
    bvec = [np.asarray(v) for v in [L0_attn_bq, L0_attn_bk, L0_attn_bv, L0_attn_bg]]
    gamma = np.asarray(L0_attn_gamma)
    Wf = np.asarray(L0_ffn_weight)
    bf_ = np.asarray(L0_ffn_bias)

    in_maps = [_core_inputs(c, x, W, bvec, gamma, Wf, bf_) for c in range(NCORES)]
    if 'nc' not in _NC_CACHE:
        _NC_CACHE['nc'] = build_nc()
    nc = _NC_CACHE['nc']
    res = run_bass_kernel_spmd(nc, in_maps, core_ids=list(range(NCORES)))
    out = np.empty((B, T, E), np.float32)
    for c in range(NCORES):
        b, half = c // 2, c % 2
        out[b, half * 2048:(half + 1) * 2048] = res.results[c]['out']
    return out
